# version 25
# baseline (speedup 1.0000x reference)
import numpy as np
import ml_dtypes
from contextlib import ExitStack

import concourse.bass as bass
import concourse.mybir as mybir
from concourse.bass_utils import run_bass_kernel_spmd

BF = ml_dtypes.bfloat16
F32 = mybir.dt.float32
BF16 = mybir.dt.bfloat16
AF = mybir.ActivationFunctionType
ALU = mybir.AluOpType
AX = mybir.AxisListType

N_CORES = 8
TC = 4096
D = 1024
NG = 8
EPS = 1e-5
LN8 = float(np.log(8.0))
C_INV = float(-12.5 * np.log(2.0))
S_SCALE = float(2.0 ** -25)
P1SUB = 99
P1NOSQ = False
P1NOEVAC = False
P1NOMM = False
P1QK = 2
P1O = 8


def build_nc(p2_stages=NG, p1_stages=NG):
    nc = bass.Bass("TRN2", target_bir_lowering=False)

    x_d = nc.dram_tensor("x", [TC, D], F32, kind="ExternalInput").ap()
    wq_d = nc.dram_tensor("wqt", [8, 128, D], F32, kind="ExternalInput").ap()
    wk_d = nc.dram_tensor("wkt", [8, 128, D], F32, kind="ExternalInput").ap()
    wv_d = nc.dram_tensor("wvt", [8, 128, D], F32, kind="ExternalInput").ap()
    wf_d = nc.dram_tensor("wft", [8, 128, D], F32, kind="ExternalInput").ap()
    bd_d = nc.dram_tensor("bd", [8, 128, 128], F32, kind="ExternalInput").ap()
    bq_d = nc.dram_tensor("bqc", [128, 8], F32, kind="ExternalInput").ap()
    bk_d = nc.dram_tensor("bkc", [128, 8], F32, kind="ExternalInput").ap()
    bv_d = nc.dram_tensor("bvr", [1, D], F32, kind="ExternalInput").ap()
    fs_d = nc.dram_tensor("fsc", [128, 8], F32, kind="ExternalInput").ap()
    fb_d = nc.dram_tensor("fbc", [128, 8], F32, kind="ExternalInput").ap()
    e16_d = nc.dram_tensor("e16", [16, D], BF16, kind="ExternalInput").ap()
    ob_d = nc.dram_tensor("onesblk", [8, 128, 16], BF16, kind="ExternalInput").ap()
    or_d = nc.dram_tensor("onesrow", [1, 128], BF16, kind="ExternalInput").ap()
    o128_d = nc.dram_tensor("ones128", [128, 1], F32, kind="ExternalInput").ap()
    idf_d = nc.dram_tensor("idf", [128, 128], F32, kind="ExternalInput").ap()
    cv_d = nc.dram_tensor("cvec", [128, 4], F32, kind="ExternalInput").ap()

    out_d = nc.dram_tensor("out", [TC, D], F32, kind="ExternalOutput").ap()

    zq_h = nc.dram_tensor("zq_h", [NG, 128, 8, 512], BF16, kind="Internal").ap()
    zl_h = nc.dram_tensor("zl_h", [NG, 128, 8, 512], BF16, kind="Internal").ap()
    zk_h = nc.dram_tensor("zk_h", [NG, 128, 8, 512], BF16, kind="Internal").ap()
    cc_in = nc.dram_tensor("cc_in", [1, 2], F32, kind="Internal")
    cc_out = nc.dram_tensor("cc_out", [1, 2], F32, kind="Internal")

    st = ExitStack()

    def sbt(name, shape, dtype):
        return st.enter_context(nc.sbuf_tensor(name, shape, dtype))

    warena = sbt("warena", [128, 3 * 8192], BF16)
    x_pool = sbt("x_pool", [128, 2, 4, D], F32)
    ln0_p = sbt("ln0_p", [128, 2, D], F32)
    ln0T = sbt("ln0T", [128, 8, 512], BF16)
    qT = sbt("qT", [128, 2, 8, 512], BF16)
    z_p = sbt("z_p", [128, 2, 8, 512], BF16)
    prod_p = sbt("prod_p", [128, 2, 512], BF16)
    rcp_sb = sbt("rcp_sb", [16, 512], F32)
    rbf_sb = sbt("rbf_sb", [16, 512], BF16)
    rexp_p = sbt("rexp_p", [128, 1, D], BF16)
    zt1 = sbt("zt1", [128, 1], F32)
    zt2 = sbt("zt2", [128, 2, 1024], F32)
    lnssum = sbt("lnssum", [128, 1], F32)
    ratio_sb = sbt("ratio_sb", [128, 1], F32)
    attn_p = sbt("attn_p", [128, 4, D], F32)
    y1_p = sbt("y1_p", [128, 4, D], F32)
    y_p = sbt("y_p", [128, 8, 512], F32)
    l4t_p = sbt("l4t_p", [128, 2, D], F32)
    bd_sb = sbt("bd_sb", [128, 8, 128], BF16)
    e16_sb = sbt("e16_sb", [16, D], BF16)
    ob_sb = sbt("ob_sb", [128, 8, 16], BF16)
    or_sb = sbt("or_sb", [1, 128], BF16)
    o128_sb = sbt("o128_sb", [128, 1], F32)
    idf_sb = sbt("idf_sb", [128, 128], F32)
    bq_sb = sbt("bq_sb", [128, 8], F32)
    bk_sb = sbt("bk_sb", [128, 8], F32)
    bv_sb = sbt("bv_sb", [1, D], BF16)
    fs_sb = sbt("fs_sb", [128, 8], F32)
    fb_sb = sbt("fb_sb", [128, 8], F32)
    cv_sb = sbt("cv_sb", [128, 4], F32)
    bn6 = sbt("bn6", [128, 4, 12], F32)
    mv_sb = sbt("mv_sb", [128, 4, 2], F32)
    lnt4 = sbt("lnt4", [128, 4], F32)
    rstd4 = sbt("rstd4", [128, 4], F32)
    mr4 = sbt("mr4", [128, 4], F32)
    ngm4 = sbt("ngm4", [128, 4], F32)
    r1_all = sbt("r1_all", [128, 32], F32)
    n1_all = sbt("n1_all", [128, 32], F32)
    ssq_c = sbt("ssq_c", [128, 2, 64], F32)
    ssqv = sbt("ssqv", [128, 2], F32)
    pair_sb = sbt("pair_sb", [1, 2], F32)
    cc_sb = sbt("cc_sb", [128, 2], F32)
    t2_sb = sbt("t2_sb", [128, 2], F32)
    inv_sb = sbt("inv_sb", [128, 2], F32)
    sq1_sb = sbt("sq1_sb", [128, 2], F32)
    lnsc_sb = sbt("lnsc_sb", [128, 2], F32)

    psA = st.enter_context(nc.psum_tensor("psA", [128, 2, 1024], F32))
    psB = st.enter_context(nc.psum_tensor("psB", [128, 2, 1024], F32))

    class Sem:
        def __init__(self, h):
            self.h = h
            self.n = 0

    names = ("DC", "DW", "DF", "XS0", "XS1", "ZS0", "ZS1", "ZL", "ZL2",
             "OS0", "OS1", "OS2", "OS3", "TE", "VE", "SE", "GE", "CC")
    sems = {nm: Sem(st.enter_context(nc.semaphore(nm))) for nm in names}
    DC, DW, DF = sems["DC"], sems["DW"], sems["DF"]
    ZL, ZL2 = sems["ZL"], sems["ZL2"]
    XS = [sems["XS0"], sems["XS1"]]
    ZS = [sems["ZS0"], sems["ZS1"]]
    OS = [sems["OS0"], sems["OS1"], sems["OS2"], sems["OS3"]]
    TE, VE, SE, GE, CC = sems["TE"], sems["VE"], sems["SE"], sems["GE"], sems["CC"]

    ops = {k: [] for k in ("sync", "gpsimd", "vector", "scalar", "tensor")}

    def E(eng, f):
        ops[eng].append(f)

    def W(eng, sem, val):
        if val and val > 0:
            E(eng, lambda e, s=sem.h, v=val: e.wait_ge(s, v))

    def EI(eng, f, sem=None, k=1):
        if sem is None:
            E(eng, f)
            return None
        sem.n += k
        E(eng, lambda e, f=f, s=sem.h, k=k: f(e).then_inc(s, k))
        return sem.n

    def INC(eng, sem, k=1):
        f = ops[eng].pop()
        sem.n += k
        ops[eng].append(lambda e, f=f, s=sem.h, k=k: f(e).then_inc(s, k))
        return sem.n

    def VSELF():
        n = INC("vector", VE)
        W("vector", VE, n)
        return n

    def w_lhs(slot, c, o):
        base = slot * 8192 + c * 1024 + o * 128
        return warena[:, base:base + 128]

    def w_mov(slot, c, lo, width):
        base = slot * 8192 + c * 1024 + lo
        return warena[:, base:base + width]

    for src, dstslot in ((wq_d, 0), (wk_d, 1), (wv_d, 2)):
        dst = warena[:, dstslot * 8192:(dstslot + 1) * 8192].rearrange(
            "p (c f) -> p c f", c=8)
        EI("gpsimd", lambda e, d=dst, s=src: e.dma_start(
            out=d, in_=s.rearrange("c p f -> p c f"), max_dma_last_dim=4096),
           DW, 16)
    EI("gpsimd", lambda e: e.dma_start(
        out=bd_sb[:, :, :], in_=bd_d.rearrange("c p f -> p c f"),
        max_dma_last_dim=4096), DW, 16)
    EI("gpsimd", lambda e: e.dma_start(out=bv_sb[:, :], in_=bv_d,
                                       max_dma_last_dim=4096), DW, 16)
    DW_WEIGHTS = DW.n

    for dst, src in ((e16_sb, e16_d), (ob_sb, ob_d.rearrange("c p f -> p c f")),
                     (or_sb, or_d), (o128_sb, o128_d), (idf_sb, idf_d),
                     (bq_sb, bq_d), (bk_sb, bk_d), (fs_sb, fs_d), (fb_sb, fb_d),
                     (cv_sb, cv_d)):
        EI("sync", lambda e, d=dst, s=src: e.dma_start(out=d[...], in_=s), DC, 16)
    DX_CONSTS = DC.n

    x_load_done = [None] * 16
    x_free_gate = [None] * 16

    def emit_x_load(i, g):
        if i >= 2 and x_free_gate[i - 2] is not None:
            gates = x_free_gate[i - 2]
            if isinstance(gates, tuple):
                gates = [gates]
            for sm, v in gates:
                W("sync", sm, v)
        src = x_d[g * 512:(g + 1) * 512, :].rearrange("(b p) f -> p b f", p=128)
        sem = XS[g % 2]
        EI("sync", lambda e, s=src, sl=(g % 2): e.dma_start(
            out=x_pool[:, sl, :, :], in_=s), sem, 16)
        x_load_done[i] = (sem, sem.n)

    emit_x_load(0, 0)
    emit_x_load(1, 1)
    W("scalar", DC, DX_CONSTS)
    W("vector", DC, DX_CONSTS)

    tr_psB_free = [0, 0]
    tr_te = {}
    ln0T_ve = [0, 0, 0, 0]
    qk_gate = {i: (0, 0) for i in range(4)}
    z_gate = {0: 0, 1: 0, 2: 0, 3: 0}
    zdma_done = [0, 0]
    exp_se_stage = [0] * NG

    def emit_ln1_chain(gg, half):
        sl = gg % 2
        b0 = 2 * half
        cols = slice(gg * 4 + b0, gg * 4 + b0 + 2)
        W("vector", *x_load_done[gg])
        W("vector", SE, SE.n)
        for b in (b0, b0 + 1):
            for h in range(2):
                EI("vector", lambda e, b=b, h=h, sl=sl: e.bn_stats(
                    bn6[:, b, 6 * h:6 * h + 6],
                    x_pool[:, sl, b, 512 * h:512 * h + 512]))
        VSELF()
        for b in (b0, b0 + 1):
            EI("vector", lambda e, b=b: e.bn_aggr(mv_sb[:, b, :], bn6[:, b, :]))
        ve_aggr = INC("vector", VE)
        W("scalar", VE, ve_aggr)
        EI("scalar", lambda e, b0=b0: e.activation(
            lnt4[:, b0:b0 + 2], mv_sb[:, b0:b0 + 2, 1:2], AF.Ln,
            bias=cv_sb[:, 0:1]))
        W("scalar", SE, INC("scalar", SE))
        se_r1 = EI("scalar", lambda e, b0=b0, cols=cols: e.activation(
            r1_all[:, cols], lnt4[:, b0:b0 + 2], AF.Exp, scale=-0.5), SE)
        W("vector", SE, se_r1)
        EI("vector", lambda e, b0=b0, cols=cols: e.tensor_mul(
            mr4[:, b0:b0 + 2], mv_sb[:, b0:b0 + 2, 0:1], r1_all[:, cols]), VE)
        W("vector", VE, VE.n)
        return EI("vector", lambda e, b0=b0, cols=cols: e.tensor_scalar_mul(
            n1_all[:, cols], mr4[:, b0:b0 + 2], -1.0), VE)

    def emit_collect():
        W("vector", SE, SE.n)
        for j in range(2):
            EI("vector", lambda e, j=j: e.tensor_reduce(
                ssqv[:, j:j + 1], ssq_c[:, j, :], AX.X, ALU.add))
        ve_ssqv = INC("vector", VE)
        W("tensor", VE, ve_ssqv)
        W("tensor", SE, SE.n)
        EI("tensor", lambda e: e.matmul(
            psA[0:1, 0, 512:514], o128_sb[:, :], ssqv[:, :],
            start=True, stop=True))
        te_pair = INC("tensor", TE)
        W("vector", TE, te_pair)
        ve_pair_ = EI("vector", lambda e: e.tensor_copy(
            pair_sb[0:1, :], psA[0:1, 0, 512:514]), VE)
        W("gpsimd", VE, ve_pair_)
        EI("gpsimd", lambda e: e.dma_start(out=cc_in[:, :], in_=pair_sb[:, :]),
           DW, 16)
        W("gpsimd", DW, DW.n)
        EI("gpsimd", lambda e: e.collective_compute(
            "AllReduce", ALU.add, replica_groups=[list(range(N_CORES))],
            ins=[cc_in.ap().opt()], outs=[cc_out.ap().opt()]), CC, 1)
        W("gpsimd", CC, CC.n)
        EI("gpsimd", lambda e: e.dma_start(
            out=cc_sb[:, :], in_=cc_out.ap().broadcast_to([128, 2])), DW, 16)
        return ve_pair_, DW.n

    cc_results = None

    for g in range(p1_stages):
        slot = g % 2
        g4 = g * 4
        ve_ng_h = [emit_ln1_chain(g, 0), emit_ln1_chain(g, 1)]

        for b in range(4):
            t = g4 + b
            W("scalar", VE, ve_ng_h[b // 2])
            if t >= 2:
                W("scalar", TE, tr_te.get(t - 2, 0))
            apl = EI("scalar", lambda e, b=b, t=t, sl=slot: e.activation(
                ln0_p[:, b % 2, :], x_pool[:, sl, b, :], AF.Identity,
                scale=r1_all[:, t:t + 1], bias=n1_all[:, t:t + 1]), SE)
            W("tensor", SE, apl)
            W("tensor", VE, tr_psB_free[b % 2])
            for zi in (b % 2, b % 2 + 2):
                zg = z_gate[zi]
                if isinstance(zg, tuple):
                    W("tensor", zg[0], zg[1])
                else:
                    W("tensor", VE, zg)
            if t == 0:
                W("tensor", DC, DX_CONSTS)
            for c in range(8):
                EI("tensor", lambda e, b=b, c=c: e.transpose(
                    psB[:, b % 2, c * 128:(c + 1) * 128],
                    ln0_p[:, b % 2, c * 128:(c + 1) * 128], idf_sb[:, :]))
            tr_te[t] = INC("tensor", TE)
            W("vector", TE, tr_te[t])
            if b == 0:
                W("vector", ZL, ZL.n)
            src = psB[:, b % 2, :].rearrange("p (c f) -> p c f", c=8)
            dst = ln0T[:, :, b * 128:(b + 1) * 128]
            n = EI("vector", lambda e, s=src, d=dst: e.tensor_copy(d, s), VE)
            tr_psB_free[b % 2] = n
            ln0T_ve[b] = n
        ve_ln0T = ln0T_ve[3]
        W("sync", VE, ve_ln0T)
        EI("sync", lambda e, g=g: e.dma_start(out=zl_h[g, :, :, :],
                                              in_=ln0T[:, :, :]), ZL, 16)

        x_free_gate[g] = (SE, SE.n)
        pslice = psA[:, :, :].rearrange("p a b -> p (a b)")
        pBflat = psB[:, :, :].rearrange("p a b -> p (a b)")
        if P1SUB < 4:
            continue
        qT_evac = [0] * 8
        for qk in range(P1QK):
            wslot, bias, qslot, ssq_j = ((0, bq_sb, 0, 0), (1, bk_sb, 1, 1))[qk]
            W("tensor", VE, ve_ln0T)
            W("tensor", VE, VE.n)
            W("tensor", SE, SE.n)
            if g == 0 and qk == 0:
                W("tensor", DW, DW_WEIGHTS)
            for o in range(P1O):
                qa = 1024 * (o % 2) + 512 * ((o // 2) % 2)
                gv, gs = qk_gate[o % 4]
                W("tensor", VE, gv)
                W("tensor", SE, gs)
                pslice = psA[:, :, :].rearrange("p a b -> p (a b)")
                for c in range(8):
                    EI("tensor", lambda e, ws=wslot, c=c, o=o, qa=qa, ps=pslice:
                       e.matmul(ps[:, qa:qa + 512], w_lhs(ws, c, o),
                                ln0T[:, c, :], start=(c == 0), stop=(c == 7)))
                te_o = INC("tensor", TE)
                W("vector", TE, te_o)
                ve_o = EI("vector", lambda e, qs=qslot, o=o, bb=bias, qa=qa,
                          ps=pslice: e.tensor_scalar(
                    qT[:, qs, o, :], ps[:, qa:qa + 512], bb[:, o:o + 1], None,
                    ALU.add), VE)
                qT_evac[o] = ve_o
                W("scalar", TE, te_o)
                W("scalar", VE, ve_o)
                l4f = l4t_p[:, :, :].rearrange("p a b -> p (a b)")
                se_o = EI("scalar", lambda e, o=o, g=g, j=ssq_j, bb=bias, qa=qa,
                          ps=pslice, l4f=l4f: e.activation(
                    l4f[:, qa:qa + 512], ps[:, qa:qa + 512], AF.Square,
                    bias=bb[:, o:o + 1],
                    accum_out=ssq_c[:, j, g * 8 + o:g * 8 + o + 1]), SE)
                qk_gate[o % 4] = (ve_o, se_o)

            if qk == 1 and g == NG - 1:
                cc_results = emit_collect()
            if g == 0 and qk == 0:
                pass
            for c in range(8):
                za = 1024 * (c % 2) + 512 * ((c // 2) % 2)
                zg = z_gate[c % 4]
                if isinstance(zg, tuple):
                    W("tensor", zg[0], zg[1])
                else:
                    W("tensor", VE, zg)
                W("tensor", VE, qT_evac[c])
                EI("tensor", lambda e, c=c, qs=qslot, za=za: e.matmul(
                    pBflat[:, za:za + 512], bd_sb[:, c, :], qT[:, qs, c, :],
                    start=True, stop=True))
                te_z = INC("tensor", TE)
                if c % 2 == 0:
                    W("vector", TE, te_z)
                    W("vector", ZS[qk], zdma_done[qk])
                    z_gate[c % 4] = (VE, EI(
                        "vector", lambda e, c=c, qk=qk, za=za: e.tensor_copy(
                            z_p[:, qk, c, :], pBflat[:, za:za + 512]), VE))
                else:
                    W("scalar", TE, te_z)
                    W("scalar", ZS[qk], zdma_done[qk])
                    z_gate[c % 4] = (SE, EI(
                        "scalar", lambda e, c=c, qk=qk, za=za: e.activation(
                            z_p[:, qk, c, :], pBflat[:, za:za + 512],
                            AF.Identity), SE))
            W("sync", VE, VE.n)
            W("sync", SE, SE.n)
            EI("sync", lambda e, g=g, h=(zq_h if qk == 0 else zk_h), qk=qk:
               e.dma_start(out=h[g, :, :, :], in_=z_p[:, qk, :, :]), ZS[qk], 16)
            zdma_done[qk] = ZS[qk].n

        if g + 2 < p1_stages:
            emit_x_load(g + 2, g + 2)

    P1_TE = TE.n
    P1_SE = SE.n
    P1_ZVE = max((v[1] if isinstance(v, tuple) else v)
                 for k, v in z_gate.items()
                 if not isinstance(v, tuple) or v[0] is VE) if True else 0
    P1_ZSE = max(((v[1] if isinstance(v, tuple) else 0)
                  for v in z_gate.values() if isinstance(v, tuple) and v[0] is SE),
                 default=0)

    ve_pair, DW_CC = cc_results

    wf_stage_aps = [y_p[:, :, :].rearrange("p a b -> p (a b)"),
                    x_pool[:, 0, :, :].rearrange("p a b -> p (a b)")]
    ve_sign = [0, 0]
    for rnd in range(2):
        wfv = wf_stage_aps[rnd]
        if rnd == 1:
            W("sync", *x_free_gate[6])
        src = wf_d[rnd * 4:(rnd + 1) * 4, :, :].rearrange("c p f -> p c f")
        dst = wfv.rearrange("p (c f) -> p c f", c=4)
        if rnd == 1:
            W("sync", DF, DF.n)
        EI("sync", lambda e, s=src, d=dst: e.dma_start(out=d, in_=s), DF, 16)
        W("vector", DF, DF.n)
        if rnd == 0:
            W("vector", TE, P1_TE)
        for c in range(4):
            EI("vector", lambda e, c=c, wfv=wfv: e.tensor_single_scalar(
                wfv[:, c * 1024:(c + 1) * 1024], wfv[:, c * 1024:(c + 1) * 1024],
                0.0, ALU.is_ge))
        VSELF()
        for c in range(4):
            cc_i = rnd * 4 + c
            EI("vector", lambda e, c=c, cc_i=cc_i, wfv=wfv: e.tensor_scalar(
                warena[:, cc_i * 1024:(cc_i + 1) * 1024],
                wfv[:, c * 1024:(c + 1) * 1024],
                2.0, -1.0, ALU.mult, ALU.add))
        ve_sign[rnd] = INC("vector", VE)
    VE_SIGN = ve_sign[1]

    W("vector", DW, DW_CC)
    EI("vector", lambda e: e.tensor_scalar_mul(t2_sb[:, :], cc_sb[:, :], S_SCALE),
       VE)
    ve_t2 = VE.n
    W("scalar", VE, ve_t2)
    EI("scalar", lambda e: e.activation(sq1_sb[:, :], t2_sb[:, :], AF.Ln))
    W("scalar", SE, INC("scalar", SE))
    se_inv = EI("scalar", lambda e: e.activation(
        inv_sb[:, :], sq1_sb[:, :], AF.Exp, scale=-0.5, bias=cv_sb[:, 1:2]), SE)
    W("vector", SE, se_inv)
    EI("vector", lambda e: e.tensor_mul(sq1_sb[:, :], inv_sb[:, :], inv_sb[:, :]),
       VE)
    W("vector", VE, VE.n)
    EI("vector", lambda e: e.tensor_mul(sq1_sb[:, :], sq1_sb[:, :], cc_sb[:, :]),
       VE)
    W("vector", VE, VE.n)
    ve_lnsc = EI("vector", lambda e: e.tensor_scalar(
        lnsc_sb[:, :], sq1_sb[:, :], -0.5, -LN8, ALU.mult, ALU.add), VE)
    W("vector", VE, ve_lnsc)
    EI("vector", lambda e: e.tensor_add(
        lnssum[:, :], lnsc_sb[:, 0:1], lnsc_sb[:, 1:2]), VE)
    W("vector", VE, VE.n)
    EI("vector", lambda e: e.reciprocal(zt1[:, 0:1], inv_sb[:, 0:1]), VE)
    W("vector", VE, VE.n)
    ve_lnsc = EI("vector", lambda e: e.tensor_mul(
        ratio_sb[:, :], inv_sb[:, 1:2], zt1[:, 0:1]), VE)

    z_load_done = [None] * NG
    zread_ve = [0] * NG
    ffn_end_te = 0
    v_end_te = 0
    s_te_prev = [0] * 8
    attn_ve = [0, 0]
    attn_prev_ve = 0
    rexp_ve = [0, 0]
    epi_se = [0, 0, 0, 0]
    l4_ge = [0, 0]
    l4_ve = [0, 0]
    ln4_blk_se = [0, 0, 0, 0]
    ln4_blk_ve = [0, 0, 0, 0]
    prod_last = 0
    attn_te = [0, 0, 0, 0]
    y1_do = {}

    def emit_z_load(g):
        W("sync", VE, zread_ve[g - 1] if g > 0 else 0)
        if g == 0:
            W("sync", ZS[0], zdma_done[0])
            W("sync", ZS[1], zdma_done[1])
        EI("sync", lambda e, g=g: e.dma_start(out=z_p[:, 0, :, :],
                                              in_=zq_h[g, :, :, :]), ZS[0], 16)
        EI("sync", lambda e, g=g: e.dma_start(out=z_p[:, 1, :, :],
                                              in_=zk_h[g, :, :, :]), ZS[1], 16)
        z_load_done[g] = (ZS[0].n, ZS[1].n)

    if p2_stages > 0:
        x_free_gate[6] = (VE, VE_SIGN)
        emit_z_load(0)
        W("sync", ZL, ZL.n)
        EI("sync", lambda e: e.dma_start(out=ln0T[:, :, :],
                                         in_=zl_h[0, :, :, :]), ZL2, 16)
        emit_x_load(8, 0)

    for g in range(p2_stages):
        slot = g % 2
        g4 = g * 4
        i_x = 8 + g
        if g > 0:
            zl2_done = next_zl2
        if g + 1 < p2_stages:
            emit_x_load(8 + g + 1, g + 1)

        if g == 0:
            zl2_done = ZL2.n

        def emit_v(b):
            W("tensor", VE, attn_ve[b % 2])
            W("tensor", SE, ln4_blk_se[b % 2])
            W("tensor", VE, ln4_blk_ve[b % 2])
            W("tensor", ZL2, zl2_done)

            if g == 0 and b == 0:
                W("tensor", DW, DW_WEIGHTS)
                W("tensor", VE, P1_ZVE)
                W("tensor", SE, P1_ZSE)
                W("tensor", VE, ve_pair)
                W("tensor", SE, P1_SE)
            for h in range(2):
                for c in range(8):
                    EI("tensor", lambda e, b=b, c=c, h=h: e.matmul(
                        psA[:, b % 2, 512 * h:512 * h + 512],
                        ln0T[:, c, b * 128:(b + 1) * 128],
                        w_mov(2, c, 512 * h, 512),
                        start=(c == 0), stop=False))
                EI("tensor", lambda e, b=b, h=h: e.matmul(
                    psA[:, b % 2, 512 * h:512 * h + 512], or_sb[0:1, :],
                    bv_sb[0:1, 512 * h:512 * h + 512], start=False, stop=True))
            return INC("tensor", TE)

        v_te = [0] * 4
        v_te[0] = emit_v(0)
        v_te[1] = emit_v(1)

        W("vector", VE, ve_lnsc)
        W("vector", ZS[0], z_load_done[g][0])
        W("vector", ZS[1], z_load_done[g][1])
        W("tensor", SE, ln4_blk_se[2])
        W("tensor", VE, ln4_blk_ve[2])
        if g == 0:
            W("tensor", VE, P1_ZVE)
            W("tensor", SE, P1_ZSE)
        exp_c = [0] * 4
        t2_ve = [0] * 4
        s_te = [0] * 8
        for cp in range(4):
            if cp >= 2:
                W("vector", SE, exp_c[cp - 2])
            t2_ve[cp] = EI("vector", lambda e, cp=cp: e.scalar_tensor_tensor(
                zt2[:, cp % 2, :].rearrange("p (a b) -> p a b", a=2),
                z_p[:, 1, 2 * cp:2 * cp + 2, :], ratio_sb[:, 0:1],
                z_p[:, 0, 2 * cp:2 * cp + 2, :], ALU.mult, ALU.add), VE)
            W("scalar", VE, t2_ve[cp])
            W("scalar", TE, s_te[2 * cp - 1] if cp >= 1 else s_te_prev[7])
            exp_c[cp] = EI("scalar", lambda e, cp=cp: e.activation(
                prod_p[:, :, :].rearrange("p a b -> p (a b)"),
                zt2[:, cp % 2, :], AF.Exp,
                scale=inv_sb[:, 0:1], bias=lnssum[:, 0:1]), SE)
            for k in range(2):
                c = 2 * cp + k
                W("tensor", SE, exp_c[cp])
                EI("tensor", lambda e, c=c, k=k: e.matmul(
                    psB[0:16, 0, 0:512], ob_sb[:, c, :], prod_p[:, k, :],
                    start=(c == 0), stop=(c == 7)))
                s_te[c] = INC("tensor", TE)
        s_te_prev = s_te
        exp_se_stage[g] = SE.n
        zread_ve[g] = VE.n
        if g + 1 < p2_stages:
            emit_z_load(g + 1)

        W("vector", TE, s_te[7])
        EI("vector", lambda e: e.reciprocal(rcp_sb[:, :], psB[0:16, 0, 0:512]), VE)
        W("vector", VE, VE.n)
        ve_rbf = EI("vector", lambda e: e.tensor_mul(
            rbf_sb[:, :], psB[0:16, 0, 0:512], rcp_sb[:, :]), VE)

        re_te = [0] * 4

        def emit_re(b):
            W("tensor", VE, ve_rbf)
            W("tensor", SE, rexp_ve[(b + 1) % 2])
            W("tensor", SE, ln4_blk_se[3])
            W("tensor", VE, ln4_blk_ve[3])
            for h in range(2):
                EI("tensor", lambda e, b=b, h=h: e.matmul(
                    psB[:, 1, 512 * h:512 * h + 512],
                    rbf_sb[:, b * 128:(b + 1) * 128],
                    e16_sb[0:16, 512 * h:512 * h + 512], start=True, stop=True))
            return INC("tensor", TE)

        se_ln2 = [0] * 4
        ge_y1b = [0] * 4
        ve_ng3b = [0] * 4

        def passA(b):
            nonlocal attn_prev_ve
            re_te[b] = emit_re(b)
            W("scalar", TE, re_te[b])
            rexp_ve[b % 2] = EI("scalar", lambda e, b=b: e.activation(
                rexp_p[:, 0, :], psB[:, 1, :], AF.Identity), SE)
            W("vector", TE, v_te[b])
            W("vector", TE, attn_te[b])
            W("vector", SE, rexp_ve[b % 2])
            attn_ve[b % 2] = EI("vector", lambda e, b=b: e.tensor_mul(
                attn_p[:, b, :], psA[:, b % 2, :], rexp_p[:, 0, :]), VE)
            attn_prev_ve = attn_ve[b % 2]
            W("vector", VE, attn_ve[b % 2])
            if b + 2 < 4:
                v_te[b + 2] = emit_v(b + 2)
            for h in range(2):
                EI("vector", lambda e, b=b, h=h: e.bn_stats(
                    bn6[:, b, 6 * h:6 * h + 6],
                    attn_p[:, b, 512 * h:512 * h + 512]))
            VSELF()
            ve_ag2 = EI("vector", lambda e, b=b: e.bn_aggr(
                mv_sb[:, b, :], bn6[:, b, :]), VE)
            W("scalar", VE, ve_ag2)
            EI("scalar", lambda e, b=b: e.activation(
                lnt4[:, b:b + 1], mv_sb[:, b, 1:2], AF.Ln, bias=cv_sb[:, 0:1]))
            W("scalar", SE, INC("scalar", SE))
            se_r2 = EI("scalar", lambda e, b=b: e.activation(
                rstd4[:, b:b + 1], lnt4[:, b:b + 1], AF.Exp, scale=-0.5), SE)
            W("vector", SE, se_r2)
            ve_ng2 = EI("vector", lambda e, b=b: e.tensor_scalar(
                ngm4[:, b:b + 1], mv_sb[:, b, 0:1], rstd4[:, b:b + 1], -1.0,
                ALU.mult, ALU.mult), VE)
            W("scalar", VE, ve_ng2)
            W("scalar", GE, l4_ge[b % 2])
            W("scalar", VE, l4_ve[b % 2])
            se_l2a = EI("scalar", lambda e, b=b: e.activation(
                l4t_p[:, b % 2, :], attn_p[:, b, :], AF.Identity,
                scale=rstd4[:, b:b + 1], bias=ngm4[:, b:b + 1]), SE)
            W("gpsimd", SE, se_l2a)
            W("gpsimd", *x_load_done[i_x])
            W("gpsimd", OS[b], y1_do.get((g - 1, b), 0))
            ge_y1b[b] = EI("gpsimd", lambda e, b=b, sl=slot: e.tensor_tensor(
                y1_p[:, b, :], x_pool[:, sl, b, :],
                l4t_p[:, b % 2, :], ALU.add), GE)
            l4_ge[b % 2] = ge_y1b[b]

        def passB(b):
            W("vector", GE, ge_y1b[b])
            for h in range(2):
                EI("vector", lambda e, b=b, h=h: e.bn_stats(
                    bn6[:, b, 6 * h:6 * h + 6], y1_p[:, b, 512 * h:512 * h + 512]))
            VSELF()
            ve_ag3 = EI("vector", lambda e, b=b: e.bn_aggr(
                mv_sb[:, b, :], bn6[:, b, :]), VE)
            W("scalar", VE, ve_ag3)
            EI("scalar", lambda e, b=b: e.activation(
                lnt4[:, b:b + 1], mv_sb[:, b, 1:2], AF.Ln, bias=cv_sb[:, 0:1]))
            W("scalar", SE, INC("scalar", SE))
            se_r3 = EI("scalar", lambda e, b=b: e.activation(
                rstd4[:, b:b + 1], lnt4[:, b:b + 1], AF.Exp, scale=-0.5), SE)
            W("vector", SE, se_r3)
            ve_ng3b[b] = EI("vector", lambda e, b=b: e.tensor_scalar(
                ngm4[:, b:b + 1], mv_sb[:, b, 0:1], rstd4[:, b:b + 1], -1.0,
                ALU.mult, ALU.mult), VE)
            W("scalar", VE, ve_ng3b[b])
            se_ln2[b] = EI("scalar", lambda e, b=b: e.activation(
                attn_p[:, b, :], y1_p[:, b, :], AF.Identity,
                scale=rstd4[:, b:b + 1], bias=ngm4[:, b:b + 1]), SE)

        for b in range(4):
            passA(b)
            if b >= 1:
                passB(b - 1)
        passB(3)
        x_free_gate[i_x] = [(GE, GE.n), (VE, VE.n)]
        v_end_te = v_te[3]
        if g + 1 < p2_stages:
            W("sync", TE, v_end_te)
            EI("sync", lambda e, g=g: e.dma_start(out=ln0T[:, :, :],
                                                  in_=zl_h[g + 1, :, :, :]),
               ZL2, 16)
            next_zl2 = ZL2.n
        else:
            next_zl2 = ZL2.n

        for b in range(4):
            W("tensor", SE, se_ln2[b])
            W("tensor", VE, tr_psB_free[b % 2])
            W("tensor", VE, ve_rbf)
            W("tensor", SE, rexp_ve[1])
            W("tensor", SE, ln4_blk_se[2 + b % 2])
            W("tensor", VE, ln4_blk_ve[2 + b % 2])
            for c in range(8):
                EI("tensor", lambda e, b=b, c=c: e.transpose(
                    psB[:, b % 2, c * 128:(c + 1) * 128],
                    attn_p[:, b, c * 128:(c + 1) * 128], idf_sb[:, :]))
            te_tr2 = INC("tensor", TE)
            attn_te[b] = te_tr2
            W("vector", TE, te_tr2)
            src = psB[:, b % 2, :].rearrange("p (c f) -> p c f", c=8)
            dst = qT[:, 0, :, b * 128:(b + 1) * 128]
            tr_psB_free[b % 2] = EI("vector", lambda e, s=src, d=dst:
                                    e.tensor_copy(d, s), VE)
        ve_ln2T = tr_psB_free[1]

        W("tensor", VE, ve_ln2T)
        if g == 0:
            W("tensor", VE, VE_SIGN)
        pBf = psB[:, :, :].rearrange("p a b -> p (a b)")
        for o in range(8):
            ya = 1024 * (o % 2) + 512 * ((o // 2) % 2)
            W("tensor", SE, epi_se[o % 4])
            for c in range(8):
                EI("tensor", lambda e, c=c, o=o, ya=ya: e.matmul(
                    pBf[:, ya:ya + 512], w_lhs(0, c, o), qT[:, 0, c, :],
                    start=(c == 0), stop=(c == 7)))
            te_y = INC("tensor", TE)
            W("scalar", TE, te_y)
            epi_se[o % 4] = EI("scalar", lambda e, o=o, ya=ya: e.activation(
                y_p[:, o, :], pBf[:, ya:ya + 512], AF.Identity,
                scale=fs_sb[:, o:o + 1], bias=fb_sb[:, o:o + 1]), SE)
        ffn_end_te = TE.n

        W("vector", SE, SE.n)
        ve_clip = EI("vector", lambda e: e.tensor_scalar(
            y_p[:, :, :].rearrange("p a b -> p (a b)"),
            y_p[:, :, :].rearrange("p a b -> p (a b)"),
            100.0, -100.0, ALU.min, ALU.max), VE)

        ps_map = ((psA, 0), (psA, 1), (psB, 0), (psB, 1))
        ve_ng4b = [0] * 4

        def yT_stats(b):
            ps_y, sl_y = ps_map[b]
            W("tensor", VE, ve_clip)
            W("tensor", SE, ln4_blk_se[b])
            W("tensor", VE, ln4_blk_ve[b])
            if b < 2:
                W("tensor", VE, attn_ve[b % 2])
            for c in range(8):
                EI("tensor", lambda e, b=b, c=c, p=ps_y, sl=sl_y: e.transpose(
                    p[:, sl, c * 128:(c + 1) * 128],
                    y_p[:, c, b * 128:(b + 1) * 128], idf_sb[:, :]))
            te_yT = INC("tensor", TE)
            W("vector", TE, te_yT)
            for h in range(2):
                EI("vector", lambda e, b=b, h=h, p=ps_y, sl=sl_y: e.bn_stats(
                    bn6[:, b, 6 * h:6 * h + 6],
                    p[:, sl, 512 * h:512 * h + 512]))
            VSELF()
            ve_ag4 = EI("vector", lambda e, b=b: e.bn_aggr(
                mv_sb[:, b, :], bn6[:, b, :]), VE)
            ln4_blk_ve[b] = ve_ag4
            W("scalar", VE, ve_ag4)
            EI("scalar", lambda e, b=b: e.activation(
                lnt4[:, b:b + 1], mv_sb[:, b, 1:2], AF.Ln, bias=cv_sb[:, 0:1]))
            W("scalar", SE, INC("scalar", SE))
            se_r4 = EI("scalar", lambda e, b=b: e.activation(
                rstd4[:, b:b + 1], lnt4[:, b:b + 1], AF.Exp, scale=-0.5), SE)
            W("vector", SE, se_r4)
            ve_ng4b[b] = EI("vector", lambda e, b=b: e.tensor_scalar(
                ngm4[:, b:b + 1], mv_sb[:, b, 0:1], rstd4[:, b:b + 1], -1.0,
                ALU.mult, ALU.mult), VE)

        def yT_apply(b):
            t = g4 + b
            ps_y, sl_y = ps_map[b]
            W("scalar", VE, ve_ng4b[b])
            W("scalar", GE, l4_ge[b % 2])
            W("scalar", VE, l4_ve[b % 2])
            se_l4 = EI("scalar", lambda e, b=b, p=ps_y, sl=sl_y: e.activation(
                l4t_p[:, b % 2, :], p[:, sl, :], AF.Identity,
                scale=rstd4[:, b:b + 1], bias=ngm4[:, b:b + 1]), SE)
            ln4_blk_se[b] = se_l4
            W("vector", SE, se_l4)
            W("vector", GE, ge_y1b[b])
            ve_fin = EI("vector", lambda e, b=b: e.tensor_add(
                y1_p[:, b, :], y1_p[:, b, :], l4t_p[:, b % 2, :]), VE)
            l4_ve[b % 2] = ve_fin
            W("sync", VE, ve_fin)
            EI("sync", lambda e, t=t, b=b: e.dma_start(
                out=out_d[t * 128:(t + 1) * 128, :], in_=y1_p[:, b, :]), OS[b], 16)
            y1_do[(g, b)] = OS[b].n

        for b in range(4):
            yT_stats(b)
            if b >= 1:
                yT_apply(b - 1)
        yT_apply(3)

    for b in range(4):
        W("sync", OS[b], OS[b].n)

    with nc.Block() as block:
        @block.sync
        def _(eng):
            for f in ops["sync"]:
                f(eng)

        @block.gpsimd
        def _(eng):
            for f in ops["gpsimd"]:
                f(eng)

        @block.vector
        def _(eng):
            for f in ops["vector"]:
                f(eng)

        @block.scalar
        def _(eng):
            for f in ops["scalar"]:
                f(eng)

        @block.tensor
        def _(eng):
            for f in ops["tensor"]:
                f(eng)

    st.pop_all()
    return nc


_NC_CACHE = None


def kernel(**inputs):
    global _NC_CACHE
    f32 = np.float32
    x = np.ascontiguousarray(np.asarray(inputs["L0_attn_x"], f32).reshape(-1, D))
    Wq = np.asarray(inputs["L0_attn_Wq"], f32)
    Wk = np.asarray(inputs["L0_attn_Wk"], f32)
    Wv = np.asarray(inputs["L0_attn_Wv"], f32)
    bq = np.asarray(inputs["L0_attn_bq"], f32)
    bk = np.asarray(inputs["L0_attn_bk"], f32)
    bv = np.asarray(inputs["L0_attn_bv"], f32)
    Wp = np.asarray(inputs["L0_attn_weights"], f32)
    Wf = np.asarray(inputs["L0_ffn_weights"], f32)
    fs = np.asarray(inputs["L0_ffn_scale"], f32)
    fb = np.asarray(inputs["L0_ffn_bias"], f32)

    bd = np.zeros((8, 128, 128), f32)
    for c in range(8):
        bd[c, :64, :64] = Wp[2 * c].T
        bd[c, 64:, 64:] = Wp[2 * c + 1].T
    e16 = np.zeros((16, D), f32)
    for h in range(16):
        e16[h, h * 64:(h + 1) * 64] = 1.0
    onesblk = np.zeros((8, 128, 16), f32)
    for c in range(8):
        onesblk[c, :64, 2 * c] = 1.0
        onesblk[c, 64:, 2 * c + 1] = 1.0
    cvec = np.zeros((128, 4), f32)
    cvec[:, 0] = EPS
    cvec[:, 1] = C_INV

    base = {
        "wqt": np.ascontiguousarray(Wq.T.reshape(8, 128, D)),
        "wkt": np.ascontiguousarray(Wk.T.reshape(8, 128, D)),
        "wvt": np.ascontiguousarray(Wv.T.reshape(8, 128, D)),
        "wft": np.ascontiguousarray(Wf.T.reshape(8, 128, D)),
        "bd": bd,
        "bqc": np.ascontiguousarray(bq.reshape(8, 128).T),
        "bkc": np.ascontiguousarray(bk.reshape(8, 128).T),
        "bvr": bv.reshape(1, D),
        "fsc": np.ascontiguousarray(fs.reshape(8, 128).T),
        "fbc": np.ascontiguousarray(fb.reshape(8, 128).T),
        "e16": e16.astype(BF), "onesblk": onesblk.astype(BF),
        "onesrow": np.ones((1, 128), f32).astype(BF),
        "ones128": np.ones((128, 1), f32),
        "idf": np.eye(128, dtype=f32),
        "cvec": cvec,
    }
    in_maps = []
    for c in range(N_CORES):
        m = dict(base)
        m["x"] = np.ascontiguousarray(x[c * TC:(c + 1) * TC])
        in_maps.append(m)

    if _NC_CACHE is None:
        _NC_CACHE = build_nc()
    res = run_bass_kernel_spmd(_NC_CACHE, in_maps, core_ids=list(range(N_CORES)))
    out = np.concatenate([r["out"] for r in res.results], axis=0)
    return out.reshape(np.asarray(inputs["L0_attn_x"]).shape)


# version 26
# speedup vs baseline: 1.0412x; 1.0412x over previous
import numpy as np
import ml_dtypes
from contextlib import ExitStack

import concourse.bass as bass
import concourse.mybir as mybir
from concourse.bass_utils import run_bass_kernel_spmd

BF = ml_dtypes.bfloat16
F32 = mybir.dt.float32
BF16 = mybir.dt.bfloat16
AF = mybir.ActivationFunctionType
ALU = mybir.AluOpType
AX = mybir.AxisListType

N_CORES = 8
TC = 4096
D = 1024
NG = 8
EPS = 1e-5
LN8 = float(np.log(8.0))
C_INV = float(-12.5 * np.log(2.0))
S_SCALE = float(2.0 ** -25)
P1SUB = 99
P1NOSQ = False
P1NOEVAC = False
P1NOMM = False
P1QK = 2
P1O = 8


def build_nc(p2_stages=NG, p1_stages=NG):
    nc = bass.Bass("TRN2", target_bir_lowering=False)

    x_d = nc.dram_tensor("x", [TC, D], F32, kind="ExternalInput").ap()
    wq_d = nc.dram_tensor("wqt", [8, 128, D], F32, kind="ExternalInput").ap()
    wk_d = nc.dram_tensor("wkt", [8, 128, D], F32, kind="ExternalInput").ap()
    wv_d = nc.dram_tensor("wvt", [8, 128, D], F32, kind="ExternalInput").ap()
    wf_d = nc.dram_tensor("wft", [8, 128, D], F32, kind="ExternalInput").ap()
    bd_d = nc.dram_tensor("bd", [8, 128, 128], F32, kind="ExternalInput").ap()
    bq_d = nc.dram_tensor("bqc", [128, 8], F32, kind="ExternalInput").ap()
    bk_d = nc.dram_tensor("bkc", [128, 8], F32, kind="ExternalInput").ap()
    bv_d = nc.dram_tensor("bvr", [1, D], F32, kind="ExternalInput").ap()
    fs_d = nc.dram_tensor("fsc", [128, 8], F32, kind="ExternalInput").ap()
    fb_d = nc.dram_tensor("fbc", [128, 8], F32, kind="ExternalInput").ap()
    e16_d = nc.dram_tensor("e16", [16, D], BF16, kind="ExternalInput").ap()
    ob_d = nc.dram_tensor("onesblk", [8, 128, 16], BF16, kind="ExternalInput").ap()
    or_d = nc.dram_tensor("onesrow", [1, 128], BF16, kind="ExternalInput").ap()
    o128_d = nc.dram_tensor("ones128", [128, 1], F32, kind="ExternalInput").ap()
    idf_d = nc.dram_tensor("idf", [128, 128], F32, kind="ExternalInput").ap()
    cv_d = nc.dram_tensor("cvec", [128, 4], F32, kind="ExternalInput").ap()

    out_d = nc.dram_tensor("out", [TC, D], F32, kind="ExternalOutput").ap()

    zq_h = nc.dram_tensor("zq_h", [NG, 128, 8, 512], BF16, kind="Internal").ap()
    zl_h = nc.dram_tensor("zl_h", [NG, 128, 8, 512], BF16, kind="Internal").ap()
    zk_h = nc.dram_tensor("zk_h", [NG, 128, 8, 512], BF16, kind="Internal").ap()
    cc_in = nc.dram_tensor("cc_in", [1, 2], F32, kind="Internal")
    cc_out = nc.dram_tensor("cc_out", [1, 2], F32, kind="Internal")

    st = ExitStack()

    def sbt(name, shape, dtype):
        return st.enter_context(nc.sbuf_tensor(name, shape, dtype))

    warena = sbt("warena", [128, 3 * 8192], BF16)
    x_pool = sbt("x_pool", [128, 2, 4, D], F32)
    ln0_p = sbt("ln0_p", [128, 2, D], F32)
    ln0T = sbt("ln0T", [128, 8, 512], BF16)
    qT = sbt("qT", [128, 2, 8, 512], BF16)
    z_p = sbt("z_p", [128, 2, 8, 512], BF16)
    prod_p = sbt("prod_p", [128, 2, 512], BF16)
    rcp_sb = sbt("rcp_sb", [16, 512], F32)
    rbf_sb = sbt("rbf_sb", [16, 512], BF16)
    rexp_p = sbt("rexp_p", [128, 1, D], BF16)
    zt1 = sbt("zt1", [128, 1], F32)
    zt2 = sbt("zt2", [128, 2, 1024], F32)
    lnssum = sbt("lnssum", [128, 1], F32)
    ratio_sb = sbt("ratio_sb", [128, 1], F32)
    attn_p = sbt("attn_p", [128, 4, D], F32)
    y1_p = sbt("y1_p", [128, 4, D], F32)
    y_p = sbt("y_p", [128, 8, 512], F32)
    l4t_p = sbt("l4t_p", [128, 2, D], F32)
    bd_sb = sbt("bd_sb", [128, 8, 128], BF16)
    e16_sb = sbt("e16_sb", [16, D], BF16)
    ob_sb = sbt("ob_sb", [128, 8, 16], BF16)
    or_sb = sbt("or_sb", [1, 128], BF16)
    o128_sb = sbt("o128_sb", [128, 1], F32)
    idf_sb = sbt("idf_sb", [128, 128], F32)
    bq_sb = sbt("bq_sb", [128, 8], F32)
    bk_sb = sbt("bk_sb", [128, 8], F32)
    bv_sb = sbt("bv_sb", [1, D], BF16)
    fs_sb = sbt("fs_sb", [128, 8], F32)
    fb_sb = sbt("fb_sb", [128, 8], F32)
    cv_sb = sbt("cv_sb", [128, 4], F32)
    bn6 = sbt("bn6", [128, 4, 12], F32)
    mv_sb = sbt("mv_sb", [128, 4, 2], F32)
    lnt4 = sbt("lnt4", [128, 4], F32)
    rstd4 = sbt("rstd4", [128, 4], F32)
    mr4 = sbt("mr4", [128, 4], F32)
    ngm4 = sbt("ngm4", [128, 4], F32)
    r1_all = sbt("r1_all", [128, 32], F32)
    n1_all = sbt("n1_all", [128, 32], F32)
    ssq_c = sbt("ssq_c", [128, 2, 64], F32)
    ssqv = sbt("ssqv", [128, 2], F32)
    pair_sb = sbt("pair_sb", [1, 2], F32)
    cc_sb = sbt("cc_sb", [128, 2], F32)
    t2_sb = sbt("t2_sb", [128, 2], F32)
    inv_sb = sbt("inv_sb", [128, 2], F32)
    sq1_sb = sbt("sq1_sb", [128, 2], F32)
    lnsc_sb = sbt("lnsc_sb", [128, 2], F32)

    psA = st.enter_context(nc.psum_tensor("psA", [128, 2, 1024], F32))
    psB = st.enter_context(nc.psum_tensor("psB", [128, 2, 1024], F32))

    class Sem:
        def __init__(self, h):
            self.h = h
            self.n = 0

    names = ("DC", "DW", "DF", "XS0", "XS1", "ZS0", "ZS1", "ZL", "ZL2",
             "OS0", "OS1", "OS2", "OS3", "TE", "VE", "SE", "GE", "CC")
    sems = {nm: Sem(st.enter_context(nc.semaphore(nm))) for nm in names}
    DC, DW, DF = sems["DC"], sems["DW"], sems["DF"]
    ZL, ZL2 = sems["ZL"], sems["ZL2"]
    XS = [sems["XS0"], sems["XS1"]]
    ZS = [sems["ZS0"], sems["ZS1"]]
    OS = [sems["OS0"], sems["OS1"], sems["OS2"], sems["OS3"]]
    TE, VE, SE, GE, CC = sems["TE"], sems["VE"], sems["SE"], sems["GE"], sems["CC"]

    ops = {k: [] for k in ("sync", "gpsimd", "vector", "scalar", "tensor")}

    def E(eng, f):
        ops[eng].append(f)

    def W(eng, sem, val):
        if val and val > 0:
            E(eng, lambda e, s=sem.h, v=val: e.wait_ge(s, v))

    def EI(eng, f, sem=None, k=1):
        if sem is None:
            E(eng, f)
            return None
        sem.n += k
        E(eng, lambda e, f=f, s=sem.h, k=k: f(e).then_inc(s, k))
        return sem.n

    def INC(eng, sem, k=1):
        f = ops[eng].pop()
        sem.n += k
        ops[eng].append(lambda e, f=f, s=sem.h, k=k: f(e).then_inc(s, k))
        return sem.n

    def VSELF():
        n = INC("vector", VE)
        W("vector", VE, n)
        return n

    def w_lhs(slot, c, o):
        base = slot * 8192 + c * 1024 + o * 128
        return warena[:, base:base + 128]

    def w_mov(slot, c, lo, width):
        base = slot * 8192 + c * 1024 + lo
        return warena[:, base:base + width]

    for src, dstslot in ((wq_d, 0), (wk_d, 1), (wv_d, 2)):
        dst = warena[:, dstslot * 8192:(dstslot + 1) * 8192].rearrange(
            "p (c f) -> p c f", c=8)
        EI("gpsimd", lambda e, d=dst, s=src: e.dma_start(
            out=d, in_=s.rearrange("c p f -> p c f"), max_dma_last_dim=4096),
           DW, 16)
    EI("gpsimd", lambda e: e.dma_start(
        out=bd_sb[:, :, :], in_=bd_d.rearrange("c p f -> p c f"),
        max_dma_last_dim=4096), DW, 16)
    EI("gpsimd", lambda e: e.dma_start(out=bv_sb[:, :], in_=bv_d,
                                       max_dma_last_dim=4096), DW, 16)
    DW_WEIGHTS = DW.n

    for dst, src in ((e16_sb, e16_d), (ob_sb, ob_d.rearrange("c p f -> p c f")),
                     (or_sb, or_d), (o128_sb, o128_d), (idf_sb, idf_d),
                     (bq_sb, bq_d), (bk_sb, bk_d), (fs_sb, fs_d), (fb_sb, fb_d),
                     (cv_sb, cv_d)):
        EI("sync", lambda e, d=dst, s=src: e.dma_start(out=d[...], in_=s), DC, 16)
    DX_CONSTS = DC.n

    x_load_done = [None] * 16
    x_free_gate = [None] * 16

    def emit_x_load(i, g):
        if i >= 2 and x_free_gate[i - 2] is not None:
            gates = x_free_gate[i - 2]
            if isinstance(gates, tuple):
                gates = [gates]
            for sm, v in gates:
                W("sync", sm, v)
        src = x_d[g * 512:(g + 1) * 512, :].rearrange("(b p) f -> p b f", p=128)
        sem = XS[g % 2]
        EI("sync", lambda e, s=src, sl=(g % 2): e.dma_start(
            out=x_pool[:, sl, :, :], in_=s), sem, 16)
        x_load_done[i] = (sem, sem.n)

    emit_x_load(0, 0)
    emit_x_load(1, 1)
    W("scalar", DC, DX_CONSTS)
    W("vector", DC, DX_CONSTS)

    tr_psB_free = [0, 0]
    tr_te = {}
    ln0T_ve = [0, 0, 0, 0]
    qk_gate = {i: (0, 0) for i in range(4)}
    z_gate = {0: 0, 1: 0, 2: 0, 3: 0}
    zdma_done = [0, 0]
    exp_se_stage = [0] * NG

    def emit_ln1_chain(gg, half):
        sl = gg % 2
        b0 = 2 * half
        cols = slice(gg * 4 + b0, gg * 4 + b0 + 2)
        W("vector", *x_load_done[gg])
        W("vector", SE, SE.n)
        for b in (b0, b0 + 1):
            for h in range(2):
                EI("vector", lambda e, b=b, h=h, sl=sl: e.bn_stats(
                    bn6[:, b, 6 * h:6 * h + 6],
                    x_pool[:, sl, b, 512 * h:512 * h + 512]))
        VSELF()
        for b in (b0, b0 + 1):
            EI("vector", lambda e, b=b: e.bn_aggr(mv_sb[:, b, :], bn6[:, b, :]))
        ve_aggr = INC("vector", VE)
        W("scalar", VE, ve_aggr)
        EI("scalar", lambda e, b0=b0: e.activation(
            lnt4[:, b0:b0 + 2], mv_sb[:, b0:b0 + 2, 1:2], AF.Ln,
            bias=cv_sb[:, 0:1]))
        W("scalar", SE, INC("scalar", SE))
        se_r1 = EI("scalar", lambda e, b0=b0, cols=cols: e.activation(
            r1_all[:, cols], lnt4[:, b0:b0 + 2], AF.Exp, scale=-0.5), SE)
        W("vector", SE, se_r1)
        EI("vector", lambda e, b0=b0, cols=cols: e.tensor_mul(
            mr4[:, b0:b0 + 2], mv_sb[:, b0:b0 + 2, 0:1], r1_all[:, cols]), VE)
        W("vector", VE, VE.n)
        return EI("vector", lambda e, b0=b0, cols=cols: e.tensor_scalar_mul(
            n1_all[:, cols], mr4[:, b0:b0 + 2], -1.0), VE)

    def emit_collect():
        W("vector", SE, SE.n)
        for j in range(2):
            EI("vector", lambda e, j=j: e.tensor_reduce(
                ssqv[:, j:j + 1], ssq_c[:, j, :], AX.X, ALU.add))
        ve_ssqv = INC("vector", VE)
        W("tensor", VE, ve_ssqv)
        W("tensor", SE, SE.n)
        EI("tensor", lambda e: e.matmul(
            psA[0:1, 0, 512:514], o128_sb[:, :], ssqv[:, :],
            start=True, stop=True))
        te_pair = INC("tensor", TE)
        W("vector", TE, te_pair)
        ve_pair_ = EI("vector", lambda e: e.tensor_copy(
            pair_sb[0:1, :], psA[0:1, 0, 512:514]), VE)
        W("gpsimd", VE, ve_pair_)
        EI("gpsimd", lambda e: e.dma_start(out=cc_in[:, :], in_=pair_sb[:, :]),
           DW, 16)
        W("gpsimd", DW, DW.n)
        EI("gpsimd", lambda e: e.collective_compute(
            "AllReduce", ALU.add, replica_groups=[list(range(N_CORES))],
            ins=[cc_in.ap().opt()], outs=[cc_out.ap().opt()]), CC, 1)
        W("gpsimd", CC, CC.n)
        EI("gpsimd", lambda e: e.dma_start(
            out=cc_sb[:, :], in_=cc_out.ap().broadcast_to([128, 2])), DW, 16)
        return ve_pair_, DW.n

    cc_results = None

    for g in range(p1_stages):
        slot = g % 2
        g4 = g * 4
        ve_ng_h = [emit_ln1_chain(g, 0), emit_ln1_chain(g, 1)]

        for b in range(4):
            t = g4 + b
            W("scalar", VE, ve_ng_h[b // 2])
            if t >= 2:
                W("scalar", TE, tr_te.get(t - 2, 0))
            apl = EI("scalar", lambda e, b=b, t=t, sl=slot: e.activation(
                ln0_p[:, b % 2, :], x_pool[:, sl, b, :], AF.Identity,
                scale=r1_all[:, t:t + 1], bias=n1_all[:, t:t + 1]), SE)
            W("tensor", SE, apl)
            W("tensor", VE, tr_psB_free[b % 2])
            for zi in (b % 2, b % 2 + 2):
                zg = z_gate[zi]
                if isinstance(zg, tuple):
                    W("tensor", zg[0], zg[1])
                else:
                    W("tensor", VE, zg)
            if t == 0:
                W("tensor", DC, DX_CONSTS)
            for c in range(8):
                EI("tensor", lambda e, b=b, c=c: e.transpose(
                    psB[:, b % 2, c * 128:(c + 1) * 128],
                    ln0_p[:, b % 2, c * 128:(c + 1) * 128], idf_sb[:, :]))
            tr_te[t] = INC("tensor", TE)
            W("vector", TE, tr_te[t])
            if b == 0:
                W("vector", ZL, ZL.n)
            src = psB[:, b % 2, :].rearrange("p (c f) -> p c f", c=8)
            dst = ln0T[:, :, b * 128:(b + 1) * 128]
            n = EI("vector", lambda e, s=src, d=dst: e.tensor_copy(d, s), VE)
            tr_psB_free[b % 2] = n
            ln0T_ve[b] = n
        ve_ln0T = ln0T_ve[3]
        W("sync", VE, ve_ln0T)
        EI("sync", lambda e, g=g: e.dma_start(out=zl_h[g, :, :, :],
                                              in_=ln0T[:, :, :]), ZL, 16)

        x_free_gate[g] = (SE, SE.n)
        pslice = psA[:, :, :].rearrange("p a b -> p (a b)")
        pBflat = psB[:, :, :].rearrange("p a b -> p (a b)")
        if P1SUB < 4:
            continue
        qT_evac = [0] * 8
        for qk in range(P1QK):
            wslot, bias, qslot, ssq_j = ((0, bq_sb, 0, 0), (1, bk_sb, 1, 1))[qk]
            W("tensor", VE, ve_ln0T)
            W("tensor", VE, VE.n)
            W("tensor", SE, SE.n)
            if g == 0 and qk == 0:
                W("tensor", DW, DW_WEIGHTS)
            for o in range(P1O):
                qa = 1024 * (o % 2) + 512 * ((o // 2) % 2)
                gv, gs = qk_gate[o % 4]
                W("tensor", VE, gv)
                W("tensor", SE, gs)
                pslice = psA[:, :, :].rearrange("p a b -> p (a b)")
                for c in range(8):
                    EI("tensor", lambda e, ws=wslot, c=c, o=o, qa=qa, ps=pslice:
                       e.matmul(ps[:, qa:qa + 512], w_lhs(ws, c, o),
                                ln0T[:, c, :], start=(c == 0), stop=(c == 7)))
                te_o = INC("tensor", TE)
                W("vector", TE, te_o)
                ve_o = EI("vector", lambda e, qs=qslot, o=o, bb=bias, qa=qa,
                          ps=pslice: e.tensor_scalar(
                    qT[:, qs, o, :], ps[:, qa:qa + 512], bb[:, o:o + 1], None,
                    ALU.add), VE)
                qT_evac[o] = ve_o
                W("scalar", TE, te_o)
                W("scalar", VE, ve_o)
                l4f = l4t_p[:, :, :].rearrange("p a b -> p (a b)")
                se_o = EI("scalar", lambda e, o=o, g=g, j=ssq_j, bb=bias, qa=qa,
                          ps=pslice, l4f=l4f: e.activation(
                    l4f[:, qa:qa + 512], ps[:, qa:qa + 512], AF.Square,
                    bias=bb[:, o:o + 1],
                    accum_out=ssq_c[:, j, g * 8 + o:g * 8 + o + 1]), SE)
                qk_gate[o % 4] = (ve_o, se_o)

            if qk == 1 and g == NG - 1:
                cc_results = emit_collect()
            if g == 0 and qk == 0:
                pass
            for c in range(8):
                za = 1024 * (c % 2) + 512 * ((c // 2) % 2)
                zg = z_gate[c % 4]
                if isinstance(zg, tuple):
                    W("tensor", zg[0], zg[1])
                else:
                    W("tensor", VE, zg)
                W("tensor", VE, qT_evac[c])
                EI("tensor", lambda e, c=c, qs=qslot, za=za: e.matmul(
                    pBflat[:, za:za + 512], bd_sb[:, c, :], qT[:, qs, c, :],
                    start=True, stop=True))
                te_z = INC("tensor", TE)
                if c % 2 == 0:
                    W("vector", TE, te_z)
                    W("vector", ZS[qk], zdma_done[qk])
                    z_gate[c % 4] = (VE, EI(
                        "vector", lambda e, c=c, qk=qk, za=za: e.tensor_copy(
                            z_p[:, qk, c, :], pBflat[:, za:za + 512]), VE))
                else:
                    W("scalar", TE, te_z)
                    W("scalar", ZS[qk], zdma_done[qk])
                    z_gate[c % 4] = (SE, EI(
                        "scalar", lambda e, c=c, qk=qk, za=za: e.activation(
                            z_p[:, qk, c, :], pBflat[:, za:za + 512],
                            AF.Identity), SE))
            W("sync", VE, VE.n)
            W("sync", SE, SE.n)
            EI("sync", lambda e, g=g, h=(zq_h if qk == 0 else zk_h), qk=qk:
               e.dma_start(out=h[g, :, :, :], in_=z_p[:, qk, :, :]), ZS[qk], 16)
            zdma_done[qk] = ZS[qk].n

        if g + 2 < p1_stages:
            emit_x_load(g + 2, g + 2)

    P1_TE = TE.n
    P1_SE = SE.n
    P1_ZVE = max((v[1] if isinstance(v, tuple) else v)
                 for k, v in z_gate.items()
                 if not isinstance(v, tuple) or v[0] is VE) if True else 0
    P1_ZSE = max(((v[1] if isinstance(v, tuple) else 0)
                  for v in z_gate.values() if isinstance(v, tuple) and v[0] is SE),
                 default=0)

    ve_pair, DW_CC = cc_results

    wf_stage_aps = [y_p[:, :, :].rearrange("p a b -> p (a b)"),
                    x_pool[:, 0, :, :].rearrange("p a b -> p (a b)")]
    ve_sign = [0, 0]
    for rnd in range(2):
        wfv = wf_stage_aps[rnd]
        if rnd == 1:
            W("sync", *x_free_gate[6])
        src = wf_d[rnd * 4:(rnd + 1) * 4, :, :].rearrange("c p f -> p c f")
        dst = wfv.rearrange("p (c f) -> p c f", c=4)
        if rnd == 1:
            W("sync", DF, DF.n)
        EI("sync", lambda e, s=src, d=dst: e.dma_start(out=d, in_=s), DF, 16)
        W("vector", DF, DF.n)
        if rnd == 0:
            W("vector", TE, P1_TE)
        for c in range(4):
            EI("vector", lambda e, c=c, wfv=wfv: e.tensor_single_scalar(
                wfv[:, c * 1024:(c + 1) * 1024], wfv[:, c * 1024:(c + 1) * 1024],
                0.0, ALU.is_ge))
        VSELF()
        for c in range(4):
            cc_i = rnd * 4 + c
            EI("vector", lambda e, c=c, cc_i=cc_i, wfv=wfv: e.tensor_scalar(
                warena[:, cc_i * 1024:(cc_i + 1) * 1024],
                wfv[:, c * 1024:(c + 1) * 1024],
                2.0, -1.0, ALU.mult, ALU.add))
        ve_sign[rnd] = INC("vector", VE)
    VE_SIGN = ve_sign[1]

    W("vector", DW, DW_CC)
    EI("vector", lambda e: e.tensor_scalar_mul(t2_sb[:, :], cc_sb[:, :], S_SCALE),
       VE)
    ve_t2 = VE.n
    W("scalar", VE, ve_t2)
    EI("scalar", lambda e: e.activation(sq1_sb[:, :], t2_sb[:, :], AF.Ln))
    W("scalar", SE, INC("scalar", SE))
    se_inv = EI("scalar", lambda e: e.activation(
        inv_sb[:, :], sq1_sb[:, :], AF.Exp, scale=-0.5, bias=cv_sb[:, 1:2]), SE)
    W("vector", SE, se_inv)
    EI("vector", lambda e: e.tensor_mul(sq1_sb[:, :], inv_sb[:, :], inv_sb[:, :]),
       VE)
    W("vector", VE, VE.n)
    EI("vector", lambda e: e.tensor_mul(sq1_sb[:, :], sq1_sb[:, :], cc_sb[:, :]),
       VE)
    W("vector", VE, VE.n)
    ve_lnsc = EI("vector", lambda e: e.tensor_scalar(
        lnsc_sb[:, :], sq1_sb[:, :], -0.5, -LN8, ALU.mult, ALU.add), VE)
    W("vector", VE, ve_lnsc)
    EI("vector", lambda e: e.tensor_add(
        lnssum[:, :], lnsc_sb[:, 0:1], lnsc_sb[:, 1:2]), VE)
    W("vector", VE, VE.n)
    EI("vector", lambda e: e.reciprocal(zt1[:, 0:1], inv_sb[:, 0:1]), VE)
    W("vector", VE, VE.n)
    ve_lnsc = EI("vector", lambda e: e.tensor_mul(
        ratio_sb[:, :], inv_sb[:, 1:2], zt1[:, 0:1]), VE)

    z_load_done = [None] * NG
    zread_ve = [0] * NG
    ffn_end_te = 0
    v_end_te = 0
    s_te_prev = [0] * 8
    attn_ve = [0, 0]
    attn_prev_ve = 0
    rexp_ve = [0, 0]
    epi_se = [0, 0, 0, 0]
    l4_ge = [0, 0]
    l4_ve = [0, 0]
    ln4_blk_se = [0, 0, 0, 0]
    ln4_blk_ve = [0, 0, 0, 0]
    prod_last = 0
    attn_te = [0, 0, 0, 0]
    y1_do = {}

    def emit_z_load(g):
        W("sync", VE, zread_ve[g - 1] if g > 0 else 0)
        if g == 0:
            W("sync", ZS[0], zdma_done[0])
            W("sync", ZS[1], zdma_done[1])
        EI("sync", lambda e, g=g: e.dma_start(out=z_p[:, 0, :, :],
                                              in_=zq_h[g, :, :, :]), ZS[0], 16)
        EI("sync", lambda e, g=g: e.dma_start(out=z_p[:, 1, :, :],
                                              in_=zk_h[g, :, :, :]), ZS[1], 16)
        z_load_done[g] = (ZS[0].n, ZS[1].n)

    if p2_stages > 0:
        x_free_gate[6] = (VE, VE_SIGN)
        emit_z_load(0)
        W("sync", ZL, ZL.n)
        EI("sync", lambda e: e.dma_start(out=ln0T[:, :, :],
                                         in_=zl_h[0, :, :, :]), ZL2, 16)
        emit_x_load(8, 0)

    for g in range(p2_stages):
        slot = g % 2
        g4 = g * 4
        i_x = 8 + g
        if g > 0:
            zl2_done = next_zl2
        if g + 1 < p2_stages:
            emit_x_load(8 + g + 1, g + 1)

        if g == 0:
            zl2_done = ZL2.n

        def emit_v(b):
            W("tensor", VE, attn_ve[b % 2])
            W("tensor", SE, ln4_blk_se[b % 2])
            W("tensor", VE, ln4_blk_ve[b % 2])
            W("tensor", ZL2, zl2_done)

            if g == 0 and b == 0:
                W("tensor", DW, DW_WEIGHTS)
                W("tensor", VE, P1_ZVE)
                W("tensor", SE, P1_ZSE)
                W("tensor", VE, ve_pair)
                W("tensor", SE, P1_SE)
            for h in range(2):
                for c in range(8):
                    EI("tensor", lambda e, b=b, c=c, h=h: e.matmul(
                        psA[:, b % 2, 512 * h:512 * h + 512],
                        ln0T[:, c, b * 128:(b + 1) * 128],
                        w_mov(2, c, 512 * h, 512),
                        start=(c == 0), stop=False))
                EI("tensor", lambda e, b=b, h=h: e.matmul(
                    psA[:, b % 2, 512 * h:512 * h + 512], or_sb[0:1, :],
                    bv_sb[0:1, 512 * h:512 * h + 512], start=False, stop=True))
            return INC("tensor", TE)

        v_te = [0] * 4
        v_te[0] = emit_v(0)
        v_te[1] = emit_v(1)

        W("vector", VE, ve_lnsc)
        W("vector", ZS[0], z_load_done[g][0])
        W("vector", ZS[1], z_load_done[g][1])
        W("tensor", SE, ln4_blk_se[2])
        W("tensor", VE, ln4_blk_ve[2])
        if g == 0:
            W("tensor", VE, P1_ZVE)
            W("tensor", SE, P1_ZSE)
        exp_c = [0] * 4
        t2_ve = [0] * 4
        s_te = [0] * 8
        for cp in range(4):
            if cp >= 2:
                W("vector", SE, exp_c[cp - 2])
            t2_ve[cp] = EI("vector", lambda e, cp=cp: e.scalar_tensor_tensor(
                zt2[:, cp % 2, :].rearrange("p (a b) -> p a b", a=2),
                z_p[:, 1, 2 * cp:2 * cp + 2, :], ratio_sb[:, 0:1],
                z_p[:, 0, 2 * cp:2 * cp + 2, :], ALU.mult, ALU.add), VE)
            W("scalar", VE, t2_ve[cp])
            W("scalar", TE, s_te[2 * cp - 1] if cp >= 1 else s_te_prev[7])
            exp_c[cp] = EI("scalar", lambda e, cp=cp: e.activation(
                prod_p[:, :, :].rearrange("p a b -> p (a b)"),
                zt2[:, cp % 2, :], AF.Exp,
                scale=inv_sb[:, 0:1], bias=lnssum[:, 0:1]), SE)
            for k in range(2):
                c = 2 * cp + k
                W("tensor", SE, exp_c[cp])
                EI("tensor", lambda e, c=c, k=k: e.matmul(
                    psB[0:16, 0, 0:512], ob_sb[:, c, :], prod_p[:, k, :],
                    start=(c == 0), stop=(c == 7)))
                s_te[c] = INC("tensor", TE)
        s_te_prev = s_te
        exp_se_stage[g] = SE.n
        zread_ve[g] = VE.n
        if g + 1 < p2_stages:
            emit_z_load(g + 1)

        W("vector", TE, s_te[7])
        EI("vector", lambda e: e.reciprocal(rcp_sb[:, :], psB[0:16, 0, 0:512]), VE)
        W("vector", VE, VE.n)
        ve_rbf = EI("vector", lambda e: e.tensor_mul(
            rbf_sb[:, :], psB[0:16, 0, 0:512], rcp_sb[:, :]), VE)

        re_te = [0] * 4

        def emit_re(b):
            W("tensor", VE, ve_rbf)
            W("tensor", VE, rexp_ve[(b + 1) % 2])
            W("tensor", SE, ln4_blk_se[3])
            W("tensor", VE, ln4_blk_ve[3])
            for h in range(2):
                EI("tensor", lambda e, b=b, h=h: e.matmul(
                    psB[:, 1, 512 * h:512 * h + 512],
                    rbf_sb[:, b * 128:(b + 1) * 128],
                    e16_sb[0:16, 512 * h:512 * h + 512], start=True, stop=True))
            return INC("tensor", TE)

        se_ln2 = [0] * 4
        ge_y1b = [0] * 4
        ve_ng3b = [0] * 4

        def passA(b):
            nonlocal attn_prev_ve
            re_te[b] = emit_re(b)
            W("vector", TE, re_te[b])
            rexp_ve[b % 2] = EI("vector", lambda e, b=b: e.tensor_copy(
                rexp_p[:, 0, :], psB[:, 1, :]), VE)
            W("vector", TE, v_te[b])
            W("vector", TE, attn_te[b])
            W("vector", VE, rexp_ve[b % 2])
            attn_ve[b % 2] = EI("vector", lambda e, b=b: e.tensor_mul(
                attn_p[:, b, :], psA[:, b % 2, :], rexp_p[:, 0, :]), VE)
            attn_prev_ve = attn_ve[b % 2]
            W("vector", VE, attn_ve[b % 2])
            if b + 2 < 4:
                v_te[b + 2] = emit_v(b + 2)
            for h in range(2):
                EI("vector", lambda e, b=b, h=h: e.bn_stats(
                    bn6[:, b, 6 * h:6 * h + 6],
                    attn_p[:, b, 512 * h:512 * h + 512]))
            VSELF()
            ve_ag2 = EI("vector", lambda e, b=b: e.bn_aggr(
                mv_sb[:, b, :], bn6[:, b, :]), VE)
            W("scalar", VE, ve_ag2)
            EI("scalar", lambda e, b=b: e.activation(
                lnt4[:, b:b + 1], mv_sb[:, b, 1:2], AF.Ln, bias=cv_sb[:, 0:1]))
            W("scalar", SE, INC("scalar", SE))
            se_r2 = EI("scalar", lambda e, b=b: e.activation(
                rstd4[:, b:b + 1], lnt4[:, b:b + 1], AF.Exp, scale=-0.5), SE)
            W("vector", SE, se_r2)
            ve_ng2 = EI("vector", lambda e, b=b: e.tensor_scalar(
                ngm4[:, b:b + 1], mv_sb[:, b, 0:1], rstd4[:, b:b + 1], -1.0,
                ALU.mult, ALU.mult), VE)
            W("scalar", VE, ve_ng2)
            W("scalar", GE, l4_ge[b % 2])
            W("scalar", VE, l4_ve[b % 2])
            se_l2a = EI("scalar", lambda e, b=b: e.activation(
                l4t_p[:, b % 2, :], attn_p[:, b, :], AF.Identity,
                scale=rstd4[:, b:b + 1], bias=ngm4[:, b:b + 1]), SE)
            W("gpsimd", SE, se_l2a)
            W("gpsimd", *x_load_done[i_x])
            W("gpsimd", OS[b], y1_do.get((g - 1, b), 0))
            ge_y1b[b] = EI("gpsimd", lambda e, b=b, sl=slot: e.tensor_tensor(
                y1_p[:, b, :], x_pool[:, sl, b, :],
                l4t_p[:, b % 2, :], ALU.add), GE)
            l4_ge[b % 2] = ge_y1b[b]

        def passB(b):
            W("vector", GE, ge_y1b[b])
            for h in range(2):
                EI("vector", lambda e, b=b, h=h: e.bn_stats(
                    bn6[:, b, 6 * h:6 * h + 6], y1_p[:, b, 512 * h:512 * h + 512]))
            VSELF()
            ve_ag3 = EI("vector", lambda e, b=b: e.bn_aggr(
                mv_sb[:, b, :], bn6[:, b, :]), VE)
            W("scalar", VE, ve_ag3)
            EI("scalar", lambda e, b=b: e.activation(
                lnt4[:, b:b + 1], mv_sb[:, b, 1:2], AF.Ln, bias=cv_sb[:, 0:1]))
            W("scalar", SE, INC("scalar", SE))
            se_r3 = EI("scalar", lambda e, b=b: e.activation(
                rstd4[:, b:b + 1], lnt4[:, b:b + 1], AF.Exp, scale=-0.5), SE)
            W("vector", SE, se_r3)
            ve_ng3b[b] = EI("vector", lambda e, b=b: e.tensor_scalar(
                ngm4[:, b:b + 1], mv_sb[:, b, 0:1], rstd4[:, b:b + 1], -1.0,
                ALU.mult, ALU.mult), VE)
            W("scalar", VE, ve_ng3b[b])
            se_ln2[b] = EI("scalar", lambda e, b=b: e.activation(
                attn_p[:, b, :], y1_p[:, b, :], AF.Identity,
                scale=rstd4[:, b:b + 1], bias=ngm4[:, b:b + 1]), SE)

        for b in range(4):
            passA(b)
            if b >= 1:
                passB(b - 1)
        passB(3)
        x_free_gate[i_x] = [(GE, GE.n), (VE, VE.n)]
        v_end_te = v_te[3]
        if g + 1 < p2_stages:
            W("sync", TE, v_end_te)
            EI("sync", lambda e, g=g: e.dma_start(out=ln0T[:, :, :],
                                                  in_=zl_h[g + 1, :, :, :]),
               ZL2, 16)
            next_zl2 = ZL2.n
        else:
            next_zl2 = ZL2.n

        for b in range(4):
            W("tensor", SE, se_ln2[b])
            W("tensor", VE, tr_psB_free[b % 2])
            W("tensor", VE, ve_rbf)
            W("tensor", VE, rexp_ve[1])
            W("tensor", SE, ln4_blk_se[2 + b % 2])
            W("tensor", VE, ln4_blk_ve[2 + b % 2])
            for c in range(8):
                EI("tensor", lambda e, b=b, c=c: e.transpose(
                    psB[:, b % 2, c * 128:(c + 1) * 128],
                    attn_p[:, b, c * 128:(c + 1) * 128], idf_sb[:, :]))
            te_tr2 = INC("tensor", TE)
            attn_te[b] = te_tr2
            W("vector", TE, te_tr2)
            src = psB[:, b % 2, :].rearrange("p (c f) -> p c f", c=8)
            dst = qT[:, 0, :, b * 128:(b + 1) * 128]
            tr_psB_free[b % 2] = EI("vector", lambda e, s=src, d=dst:
                                    e.tensor_copy(d, s), VE)
        ve_ln2T = tr_psB_free[1]

        W("tensor", VE, ve_ln2T)
        if g == 0:
            W("tensor", VE, VE_SIGN)
        pBf = psB[:, :, :].rearrange("p a b -> p (a b)")
        for o in range(8):
            ya = 1024 * (o % 2) + 512 * ((o // 2) % 2)
            W("tensor", SE, epi_se[o % 4])
            for c in range(8):
                EI("tensor", lambda e, c=c, o=o, ya=ya: e.matmul(
                    pBf[:, ya:ya + 512], w_lhs(0, c, o), qT[:, 0, c, :],
                    start=(c == 0), stop=(c == 7)))
            te_y = INC("tensor", TE)
            W("scalar", TE, te_y)
            epi_se[o % 4] = EI("scalar", lambda e, o=o, ya=ya: e.activation(
                y_p[:, o, :], pBf[:, ya:ya + 512], AF.Identity,
                scale=fs_sb[:, o:o + 1], bias=fb_sb[:, o:o + 1]), SE)
        ffn_end_te = TE.n

        W("vector", SE, SE.n)
        ve_clip = EI("vector", lambda e: e.tensor_scalar(
            y_p[:, :, :].rearrange("p a b -> p (a b)"),
            y_p[:, :, :].rearrange("p a b -> p (a b)"),
            100.0, -100.0, ALU.min, ALU.max), VE)

        ps_map = ((psA, 0), (psA, 1), (psB, 0), (psB, 1))
        ve_ng4b = [0] * 4

        def yT_stats(b):
            ps_y, sl_y = ps_map[b]
            W("tensor", VE, ve_clip)
            W("tensor", SE, ln4_blk_se[b])
            W("tensor", VE, ln4_blk_ve[b])
            if b < 2:
                W("tensor", VE, attn_ve[b % 2])
            for c in range(8):
                EI("tensor", lambda e, b=b, c=c, p=ps_y, sl=sl_y: e.transpose(
                    p[:, sl, c * 128:(c + 1) * 128],
                    y_p[:, c, b * 128:(b + 1) * 128], idf_sb[:, :]))
            te_yT = INC("tensor", TE)
            W("vector", TE, te_yT)
            for h in range(2):
                EI("vector", lambda e, b=b, h=h, p=ps_y, sl=sl_y: e.bn_stats(
                    bn6[:, b, 6 * h:6 * h + 6],
                    p[:, sl, 512 * h:512 * h + 512]))
            VSELF()
            ve_ag4 = EI("vector", lambda e, b=b: e.bn_aggr(
                mv_sb[:, b, :], bn6[:, b, :]), VE)
            ln4_blk_ve[b] = ve_ag4
            W("scalar", VE, ve_ag4)
            EI("scalar", lambda e, b=b: e.activation(
                lnt4[:, b:b + 1], mv_sb[:, b, 1:2], AF.Ln, bias=cv_sb[:, 0:1]))
            W("scalar", SE, INC("scalar", SE))
            se_r4 = EI("scalar", lambda e, b=b: e.activation(
                rstd4[:, b:b + 1], lnt4[:, b:b + 1], AF.Exp, scale=-0.5), SE)
            W("vector", SE, se_r4)
            ve_ng4b[b] = EI("vector", lambda e, b=b: e.tensor_scalar(
                ngm4[:, b:b + 1], mv_sb[:, b, 0:1], rstd4[:, b:b + 1], -1.0,
                ALU.mult, ALU.mult), VE)

        def yT_apply(b):
            t = g4 + b
            ps_y, sl_y = ps_map[b]
            W("scalar", VE, ve_ng4b[b])
            W("scalar", GE, l4_ge[b % 2])
            W("scalar", VE, l4_ve[b % 2])
            se_l4 = EI("scalar", lambda e, b=b, p=ps_y, sl=sl_y: e.activation(
                l4t_p[:, b % 2, :], p[:, sl, :], AF.Identity,
                scale=rstd4[:, b:b + 1], bias=ngm4[:, b:b + 1]), SE)
            ln4_blk_se[b] = se_l4
            W("vector", SE, se_l4)
            W("vector", GE, ge_y1b[b])
            ve_fin = EI("vector", lambda e, b=b: e.tensor_add(
                y1_p[:, b, :], y1_p[:, b, :], l4t_p[:, b % 2, :]), VE)
            l4_ve[b % 2] = ve_fin
            W("sync", VE, ve_fin)
            EI("sync", lambda e, t=t, b=b: e.dma_start(
                out=out_d[t * 128:(t + 1) * 128, :], in_=y1_p[:, b, :]), OS[b], 16)
            y1_do[(g, b)] = OS[b].n

        for b in range(4):
            yT_stats(b)
            if b >= 1:
                yT_apply(b - 1)
        yT_apply(3)

    for b in range(4):
        W("sync", OS[b], OS[b].n)

    with nc.Block() as block:
        @block.sync
        def _(eng):
            for f in ops["sync"]:
                f(eng)

        @block.gpsimd
        def _(eng):
            for f in ops["gpsimd"]:
                f(eng)

        @block.vector
        def _(eng):
            for f in ops["vector"]:
                f(eng)

        @block.scalar
        def _(eng):
            for f in ops["scalar"]:
                f(eng)

        @block.tensor
        def _(eng):
            for f in ops["tensor"]:
                f(eng)

    st.pop_all()
    return nc


_NC_CACHE = None


def kernel(**inputs):
    global _NC_CACHE
    f32 = np.float32
    x = np.ascontiguousarray(np.asarray(inputs["L0_attn_x"], f32).reshape(-1, D))
    Wq = np.asarray(inputs["L0_attn_Wq"], f32)
    Wk = np.asarray(inputs["L0_attn_Wk"], f32)
    Wv = np.asarray(inputs["L0_attn_Wv"], f32)
    bq = np.asarray(inputs["L0_attn_bq"], f32)
    bk = np.asarray(inputs["L0_attn_bk"], f32)
    bv = np.asarray(inputs["L0_attn_bv"], f32)
    Wp = np.asarray(inputs["L0_attn_weights"], f32)
    Wf = np.asarray(inputs["L0_ffn_weights"], f32)
    fs = np.asarray(inputs["L0_ffn_scale"], f32)
    fb = np.asarray(inputs["L0_ffn_bias"], f32)

    bd = np.zeros((8, 128, 128), f32)
    for c in range(8):
        bd[c, :64, :64] = Wp[2 * c].T
        bd[c, 64:, 64:] = Wp[2 * c + 1].T
    e16 = np.zeros((16, D), f32)
    for h in range(16):
        e16[h, h * 64:(h + 1) * 64] = 1.0
    onesblk = np.zeros((8, 128, 16), f32)
    for c in range(8):
        onesblk[c, :64, 2 * c] = 1.0
        onesblk[c, 64:, 2 * c + 1] = 1.0
    cvec = np.zeros((128, 4), f32)
    cvec[:, 0] = EPS
    cvec[:, 1] = C_INV

    base = {
        "wqt": np.ascontiguousarray(Wq.T.reshape(8, 128, D)),
        "wkt": np.ascontiguousarray(Wk.T.reshape(8, 128, D)),
        "wvt": np.ascontiguousarray(Wv.T.reshape(8, 128, D)),
        "wft": np.ascontiguousarray(Wf.T.reshape(8, 128, D)),
        "bd": bd,
        "bqc": np.ascontiguousarray(bq.reshape(8, 128).T),
        "bkc": np.ascontiguousarray(bk.reshape(8, 128).T),
        "bvr": bv.reshape(1, D),
        "fsc": np.ascontiguousarray(fs.reshape(8, 128).T),
        "fbc": np.ascontiguousarray(fb.reshape(8, 128).T),
        "e16": e16.astype(BF), "onesblk": onesblk.astype(BF),
        "onesrow": np.ones((1, 128), f32).astype(BF),
        "ones128": np.ones((128, 1), f32),
        "idf": np.eye(128, dtype=f32),
        "cvec": cvec,
    }
    in_maps = []
    for c in range(N_CORES):
        m = dict(base)
        m["x"] = np.ascontiguousarray(x[c * TC:(c + 1) * TC])
        in_maps.append(m)

    if _NC_CACHE is None:
        _NC_CACHE = build_nc()
    res = run_bass_kernel_spmd(_NC_CACHE, in_maps, core_ids=list(range(N_CORES)))
    out = np.concatenate([r["out"] for r in res.results], axis=0)
    return out.reshape(np.asarray(inputs["L0_attn_x"]).shape)
